# version 01
# baseline (speedup 1.0000x reference)
import contextlib
import numpy as np
import ml_dtypes
import concourse.bass as bass
import concourse.mybir as mybir
from concourse.bass_utils import run_bass_kernel_spmd

F32 = mybir.dt.float32
BF16 = mybir.dt.bfloat16
AF = mybir.ActivationFunctionType
ALU = mybir.AluOpType
AX = mybir.AxisListType

P = 128
T = 2048
NDC = 16
MAGIC = 1.5 * 2 ** 23
LAMBDA_INIT = 0.8 - 0.6 * float(np.exp(-0.3 * 12))
INV_SQRT_D = float(1.0 / np.sqrt(128.0))
NEG_BIG = -1e30
I0 = 8192
I1 = 6464
I1P = 6656
NIC0 = 64
NIC1 = 52
LAST1 = I1P - P * (NIC1 - 1)
ENGINES = ("sync", "gpsimd", "vector", "scalar", "tensor")


class Bld:
    def __init__(self, nc):
        self.nc = nc
        self.recs = {e: [] for e in ENGINES}
        self.cnt = {}
        self.owner = {}

    def tok(self, sem):
        return (sem, self.cnt.get(sem, 0))

    def op(self, eng, fn, waits=(), inc=None, amt=1, fwaits=()):
        waits = [w for w in waits if w is not None and w[1] > 0]
        fwaits = [w for w in fwaits if w is not None and w[1] > 0]
        tok = None
        if inc is not None:
            if inc not in self.cnt:
                self.cnt[inc] = 0
                self.owner[inc] = eng
            assert self.owner[inc] == eng, (inc, eng)
            self.cnt[inc] += amt
            tok = (inc, self.cnt[inc])
        self.recs[eng].append((fn, waits, fwaits, inc, amt))
        return tok

    def dma(self, eng, out, in_, sem, waits=(), fwaits=()):
        def fn(E, out=out, in_=in_):
            return E.dma_start(out=out, in_=in_)
        return self.op(eng, fn, waits=waits, inc=sem, amt=16, fwaits=fwaits)

    def finalize(self, stack):
        nc = self.nc
        sems = {n: stack.enter_context(nc.semaphore(n)) for n in self.cnt}
        block = stack.enter_context(nc.Block())
        for ename in ENGINES:
            recs = self.recs[ename]
            if not recs:
                continue

            def body(eng, recs=recs, ename=ename):
                waited = {}
                for fn, waits, fwaits, inc, amt in recs:
                    for s, v in waits:
                        if self.owner.get(s) == ename:
                            continue
                        if waited.get(s, 0) >= v:
                            continue
                        eng.wait_ge(sems[s], v)
                        waited[s] = v
                    for s, v in fwaits:
                        if waited.get(s, 0) >= v:
                            continue
                        eng.wait_ge(sems[s], v)
                        waited[s] = v
                    ins = fn(eng)
                    if inc is not None and ins is not None:
                        ins.then_inc(sems[inc], amt)

            getattr(block, ename)(body)


def strips_by_out(Wt, nk, nm):
    return np.ascontiguousarray(
        Wt.reshape(nk, 128, nm, 128).transpose(2, 1, 0, 3).reshape(nm, 128, nk * 128))


def host_prepare(inp):
    f = np.float32
    s = {}
    s["ident_b"] = np.eye(P, dtype=ml_dtypes.bfloat16)
    s["ones_row"] = np.ones((1, P), dtype=ml_dtypes.bfloat16)
    s["ones_row_f"] = np.ones((1, P), dtype=np.float32)
    s["ones_col"] = np.ones((P, 1), dtype=ml_dtypes.bfloat16)
    s["causal"] = np.triu(np.full((P, P), NEG_BIG, f), 1)
    sel = np.zeros((P, 32), dtype=ml_dtypes.bfloat16)
    for k in range(32):
        sel[64 + k, k] = 1
    s["sel32"] = sel
    A_log = np.asarray(inp["L0_attn_A_log"], f)
    s["a_log_cols"] = np.ascontiguousarray(A_log.reshape(NDC, P, 16).transpose(1, 0, 2).reshape(P, 256))
    s["d_cols"] = np.ascontiguousarray(np.asarray(inp["L0_attn_D"], f).reshape(NDC, P).T)
    s["bdt_cols"] = np.ascontiguousarray(np.asarray(inp["L0_attn_b_dt"], f).reshape(NDC, P).T)
    Wx = np.asarray(inp["L0_attn_W_x"], f)
    s["wx_t"] = np.ascontiguousarray(Wx.T.reshape(NDC, P, 96).transpose(1, 0, 2).reshape(P, NDC * 96))
    Wdt = np.asarray(inp["L0_attn_W_dt"], f)
    s["wdt_t"] = np.ascontiguousarray(Wdt.reshape(NDC, P, 64).transpose(2, 0, 1).reshape(64, T))
    s["wu_t"] = strips_by_out(np.asarray(inp["L0_ffn_W_up"], f).T, NDC, NIC0)
    s["wg_t"] = strips_by_out(np.asarray(inp["L0_ffn_W_gate"], f).T, NDC, NIC0)
    s["wd_t"] = strips_by_out(np.asarray(inp["L0_ffn_W_down"], f).T, NIC0, NDC)
    s["wq_t"] = strips_by_out(np.asarray(inp["L1_attn_Wq"], f), NDC, NDC)
    s["wk_t"] = strips_by_out(np.asarray(inp["L1_attn_Wk"], f), NDC, NDC)
    s["wo_t"] = strips_by_out(np.asarray(inp["L1_attn_Wo"], f), NDC, NDC)
    s["wv_nat"] = np.ascontiguousarray(np.asarray(inp["L1_attn_Wv"], f))
    s["lvecs"] = np.stack([np.asarray(inp[k], f) for k in
                           ("L1_attn_lambda_q1", "L1_attn_lambda_k1",
                            "L1_attn_lambda_q2", "L1_attn_lambda_k2")], 1)
    s["g_row"] = np.asarray(inp["L1_attn_g"], f).reshape(1, 256)
    def padc(a, n):
        o = np.zeros((a.shape[0], n), a.dtype); o[:, :a.shape[1]] = a; return o
    def padr(a, n):
        o = np.zeros((n, a.shape[1]), a.dtype); o[:a.shape[0], :] = a; return o
    s["f1g_nat"] = padc(np.ascontiguousarray(np.asarray(inp["L1_ffn_w_gate"], f).T), I1P)
    s["f1u_nat"] = padc(np.ascontiguousarray(np.asarray(inp["L1_ffn_w_up"], f).T), I1P)
    s["f1d_nat"] = padr(np.ascontiguousarray(np.asarray(inp["L1_ffn_w_down"], f).T), I1P)
    x = np.asarray(inp["L0_attn_x"], f)
    xT = np.ascontiguousarray(x.transpose(0, 2, 1))
    return s, xT


class Kern:
    def __init__(self, upto="f1"):
        self.upto = upto
        nc = self.nc = bass.Bass("TRN2", target_bir_lowering=False, detect_race_conditions=False)
        self.b = Bld(nc)
        d = nc.dram_tensor
        self.i = {}
        for name, shape, dt in [
            ("xT", [T, T], F32),
            ("ident_b", [P, P], BF16), ("ones_row", [1, P], BF16), ("ones_col", [P, 1], BF16),
            ("ones_row_f", [1, P], F32),
            ("causal", [P, P], F32), ("sel32", [P, 32], BF16),
            ("a_log_cols", [P, 256], F32), ("d_cols", [P, NDC], F32), ("bdt_cols", [P, NDC], F32),
            ("wx_t", [P, NDC * 96], F32), ("wdt_t", [64, T], F32),
            ("wu_t", [NIC0, P, T], F32), ("wg_t", [NIC0, P, T], F32), ("wd_t", [NDC, P, I0], F32),
            ("wq_t", [NDC, P, T], F32), ("wk_t", [NDC, P, T], F32), ("wo_t", [NDC, P, T], F32),
            ("wv_nat", [T, T], F32),
            ("lvecs", [P, 4], F32), ("g_row", [1, 256], F32),
            ("f1g_nat", [T, I1P], F32), ("f1u_nat", [T, I1P], F32), ("f1d_nat", [I1P, T], F32),
        ]:
            self.i[name] = d(name, shape, dt, kind="ExternalInput").ap()
        self.out = d("out", [T, T], F32, kind="ExternalOutput").ap()
        self.h = {}
        for name, shape in [
            ("q_wu", [NIC0, P, T]), ("q_wg", [NIC0, P, T]), ("q_wd", [NDC, P, I0]),
            ("wo_b", [NDC, P, T]),
            ("q_f1g", [13, P, 8192]), ("q_f1u", [13, P, 8192]), ("q_f1d", [I1P, T]),
            ("qspill", [NDC, P, T]), ("vspill", [NDC, P, T]), ("xqspill", [NDC, P, T]),
        ]:
            self.h[name] = d(name, shape, BF16, kind="Internal").ap()
        a = nc.alloc_sbuf_tensor
        self.STRM = a("STRM", [P, 16 * T], BF16)
        self.BIG = a("BIG", [P, 16 * T], BF16)
        self.AR = a("AR", [P, 16384], F32)
        self.ident = a("identt", [P, P], BF16)
        self.ones_row = a("ones_r", [1, P], BF16)
        self.ones_row_f = a("ones_rf", [1, P], F32)
        self.ones_col = a("ones_c", [P, 1], BF16)
        self.causal = a("causalt", [P, P], F32)
        self.sel32 = a("sel32t", [P, 32], BF16)
        self.acols = a("acolst", [P, 256], F32)
        self.d1 = a("d1t", [P, NDC], F32)
        self.bdt = a("bdtt", [P, NDC], F32)
        self.wxb = a("wxbt", [P, NDC * 96], BF16)
        self.wdtb = a("wdtbt", [64, T], BF16)
        self.consts = a("constst", [P, 8], F32)
        self.state = a("statet", [P, 256], F32)
        self.scal = a("scalt", [P, 16], F32)
        self.srinv = a("srinvt", [P, NDC], F32)
        self.fscal = a("fscalt", [P, 16], F32)
        self.g1 = a("g1t", [P, 256], F32)
        self.lam = a("lamt", [P, 2], F32)
        self.rowbuf = a("rowbuft", [P, 32], F32)
        ap_ = nc.alloc_psum_tensor
        self.ps = [ap_(f"ps{k}", [P, 512], F32) for k in range(4)]
        self.tr_idx = 0
        self.tr_evac = {}
        self.ps45 = ap_("ps45", [P, 1024], F32)
        self.psT = ap_("psTt", [P, 1024], BF16)
        self.ps7 = ap_("ps7t", [P, 512], F32)

    def transpose_pair(self, src_ap, dst_ap, waits_tr, evac_engine="scalar", kr=128, fw=()):
        b = self.b
        idx = self.tr_idx
        self.tr_idx += 1
        bank = self.psT[0:P, 0:P] if idx % 2 == 0 else self.ps7[:].bitcast(BF16)[0:P, 0:P]

        def t_fn(E, src_ap=src_ap, bank=bank, kr=kr):
            return E.transpose(bank[0:kr, :], src_ap, self.ident[:])
        w = list(waits_tr) + [self.tr_evac.get(idx - 2)]
        tt = b.op("tensor", t_fn, waits=w, inc="T")

        def e_fn(E, dst_ap=dst_ap, bank=bank, kr=kr):
            if evac_engine == "scalar":
                return E.copy(out=dst_ap, in_=bank[0:kr, :])
            return E.tensor_copy(out=dst_ap, in_=bank[0:kr, :])
        et = b.op(evac_engine, e_fn, waits=[tt], fwaits=fw,
                  inc="S" if evac_engine == "scalar" else "V")
        self.tr_evac[idx] = et
        return et

    def af(self, off_kb, ncols):
        o = int(off_kb * 256)
        assert o + ncols <= 16384, (off_kb, ncols)
        return self.AR[:, o:o + ncols]

    def ab(self, off_kb, ncols):
        o = int(off_kb * 256)
        nf = (ncols + 1) // 2
        assert o + nf <= 16384, (off_kb, ncols)
        return self.AR[:, o:o + nf].bitcast(BF16)[:, 0:ncols]

    def build(self):
        self.prologue()
        if self.upto == "pqdbg":
            self.prequant_all()
            b = self.b
            ob = self.af(0, T)
            stg2 = self.ab(8, 2048)
            for k, src_ap in [(0, self.h["q_wu"][0]), (1, self.h["q_wu"][63]),
                              (2, self.h["q_wd"][0][:, 0:2048]), (3, self.h["q_f1g"][0][:, 0:2048]),
                              (4, self.h["q_f1d"][0:128, 0:2048])]:
                b.dma("sync", stg2, src_ap, "d_pqd",
                      waits=[t for v in self.pq.values() for t in (v if isinstance(v, list) else [v])] + [b.tok("d_out")])
                lt = b.tok("d_pqd")
                def v_c(E):
                    return E.tensor_copy(out=ob[:], in_=stg2)
                vt = b.op("vector", v_c, waits=[lt, b.tok("d_out")], inc="V")
                b.dma("gpsimd", self.out[k * 128:(k + 1) * 128, 0:2048], ob[:], "d_out", waits=[vt])
            def v_s(E):
                return E.tensor_copy(out=ob[:, 0:14], in_=self.scal[:, 0:14])
            vt = b.op("vector", v_s, waits=[b.tok("d_out"), self.scal_done], inc="V")
            b.dma("gpsimd", self.out[640:768, 0:14], ob[:, 0:14], "d_out", waits=[vt])
            self.finish()
            return
        if self.upto.startswith("attn_") or self.upto in ("mdbg", "mdbg2", "mdbg3"):
            pass
        if self.upto in ("mdbg", "mdbg2", "mdbg3"):
            self.dbg_c = 0 if self.upto == "mdbg" else 1
            self.prequant_all()
            self.mamba()
            return
        self.prequant_all()
        self.mamba()
        if self.upto == "mamba":
            return self.dump_featmajor()
        self.ffn0()
        if self.upto == "f0":
            return self.dump_featmajor()
        self.attn()
        if self.upto in ("attn_probe", "attn_odump"):
            return
        if self.upto.startswith("attn"):
            return self.dump_packed() if self.upto in ("attn", "attn_ts0") else self.dump_featmajor()
        self.ffn1()
        self.finish()

    def prologue(self):
        b, i = self.b, self.i
        for dst, src in [
            (self.ident[:], i["ident_b"]), (self.ones_row[:], i["ones_row"]),
            (self.ones_col[:], i["ones_col"]), (self.causal[:], i["causal"]),
            (self.ones_row_f[:], i["ones_row_f"]),
            (self.sel32[:], i["sel32"]),
            (self.d1[:], i["d_cols"]), (self.bdt[:], i["bdt_cols"]),
        ]:
            b.dma("sync", dst, src, "d_pro")
        t_alog = self.af(0, 256)
        t_wx = self.af(2, NDC * 96)
        t_wdt = self.af(8, T)
        t_lv = self.af(16, 4)
        t_g = self.af(17, 256)
        b.dma("sync", t_alog, i["a_log_cols"], "d_pro")
        b.dma("sync", t_wx, i["wx_t"], "d_pro")
        b.dma("sync", t_wdt[0:64, :], i["wdt_t"], "d_pro")
        b.dma("sync", t_lv[:, 0:4], i["lvecs"], "d_pro")
        b.dma("sync", t_g[0:1, :], i["g_row"], "d_pro")
        ldt = b.tok("d_pro")

        def v_setup(E):
            E.memset(self.consts[:, 0:1], MAGIC)
            E.memset(self.consts[:, 1:2], -MAGIC)
            E.memset(self.consts[:, 2:3], 1.0)
            E.memset(self.consts[:, 3:4], 1e-5)
            E.memset(self.consts[:, 4:5], 1e-8)
            E.memset(self.state[:], 0.0)
            E.tensor_scalar(out=self.d1[:], in0=self.d1[:], scalar1=1.0, scalar2=None, op0=ALU.add)
            E.tensor_copy(out=self.wdtb[:], in_=t_wdt[0:64, :])
            E.tensor_mul(self.af(18, 2)[:, 0:1], t_lv[:, 0:1], t_lv[:, 1:2])
            return E.tensor_mul(self.af(18, 2)[:, 1:2], t_lv[:, 2:3], t_lv[:, 3:4])
        vt0 = b.op("vector", v_setup, waits=[ldt], inc="V")

        def v_setup2(E):
            E.tensor_copy(out=self.ab(19, 2)[:, 0:2], in_=self.af(18, 2))
            return E.tensor_copy(out=self.ab(20, 256)[0:1, :], in_=t_g[0:1, :])
        vt = b.op("vector", v_setup2, waits=[], fwaits=[vt0], inc="V")

        def s_pro(E):
            E.activation(out=self.acols[:], in_=t_alog, func=AF.Exp)
            return E.copy(out=self.wxb[:], in_=t_wx)
        st = b.op("scalar", s_pro, waits=[ldt], inc="S")

        def v_nega(E):
            return E.tensor_scalar(out=self.acols[:], in0=self.acols[:], scalar1=-1.0,
                                   scalar2=None, op0=ALU.mult)
        b.op("vector", v_nega, waits=[st], inc="V")

        def t_lam(E):
            E.matmul(self.ps7[0:1, 0:2], self.ones_col[:], self.ab(19, 2)[:, 0:2],
                     start=True, stop=True, skip_group_check=True)
            return E.matmul(self.ps7[0:P, 4:260], self.ones_row[:], self.ab(20, 256)[0:1, :],
                            start=True, stop=True, skip_group_check=True)
        tt = b.op("tensor", t_lam, waits=[vt], inc="T")

        def s_lam(E):
            return E.activation(out=self.af(18, 2)[0:1, 0:2], in_=self.ps7[0:1, 0:2], func=AF.Exp)
        st2 = b.op("scalar", s_lam, waits=[tt], inc="S")

        def v_lam(E):
            E.tensor_sub(self.af(18, 1)[0:1, 0:1], self.af(18, 2)[0:1, 0:1], self.af(18, 2)[0:1, 1:2])
            return E.tensor_scalar(out=self.g1[:], in0=self.ps7[0:P, 4:260],
                                   scalar1=1.0 - LAMBDA_INIT, scalar2=None, op0=ALU.mult)
        vt3a = b.op("vector", v_lam, waits=[st2], inc="V")

        def v_lam2(E):
            return E.tensor_scalar(out=self.af(18, 2)[0:1, 1:2], in0=self.af(18, 1)[0:1, 0:1],
                                   scalar1=LAMBDA_INIT, scalar2=None, op0=ALU.add)
        vt3b = b.op("vector", v_lam2, waits=[], fwaits=[vt3a], inc="V")

        def v_lam3(E):
            return E.tensor_copy(out=self.ab(19, 1)[0:1, 0:1], in_=self.af(18, 2)[0:1, 1:2])
        vt3 = b.op("vector", v_lam3, waits=[], fwaits=[vt3b], inc="V")

        def t_lbc(E):
            return E.matmul(self.ps7[0:P, 0:1], self.ones_row[:], self.ab(19, 1)[0:1, 0:1],
                            start=True, stop=True, skip_group_check=True)
        tt2 = b.op("tensor", t_lbc, waits=[vt3], inc="T")

        def v_lbc(E):
            E.tensor_copy(out=self.lam[:, 0:1], in_=self.ps7[0:P, 0:1])
            return E.tensor_scalar(out=self.lam[:, 1:2], in0=self.ps7[0:P, 0:1],
                                   scalar1=-1.0, scalar2=None, op0=ALU.mult)
        b.op("vector", v_lbc, waits=[tt2], inc="V")

    def prequant_one(self, idx, units, scale_mode, prev=None, nelem=None):
        b = self.b
        stg = [self.af(21, 2048), self.af(29, 2048)]
        ab_t = self.ab(37, 2048)
        qb_t = [self.ab(37, 2048), self.ab(49, 2048)]
        rt_f = self.af(41, 2048)
        ld = f"dpq{idx}"
        st = f"dpqo{idx}"
        n = len(units)
        if nelem is None:
            nelem = sum(kr * w for _, _, kr, w in units)
        abs_toks = [None] * n
        sum_toks = [None] * n
        for it, (iap, oap, kr, w) in enumerate(units):
            wts = [sum_toks[it - 2]] if it >= 2 else (list(prev) if prev else [])
            b.dma("sync", stg[it % 2][0:kr, 0:w], iap, f"{ld}{it % 2}", waits=wts)
            lt = b.tok(f"{ld}{it % 2}")

            def s_abs(E, it=it, kr=kr, w=w):
                return E.activation(out=ab_t[0:kr, 0:w], in_=stg[it % 2][0:kr, 0:w], func=AF.Abs)
            abs_toks[it] = b.op("scalar", s_abs, waits=[lt, sum_toks[it - 1] if it >= 1 else None], inc="S")
            nsl = (w + 511) // 512
            for sl in range(nsl):
                w0, w1 = sl * 512, min(w, sl * 512 + 512)

                def t_sum(E, it=it, kr=kr, w0=w0, w1=w1, first=(it == 0 and sl == 0),
                          last=(it == n - 1 and sl == nsl - 1)):
                    return E.matmul(self.ps7[0:1, 0:w1 - w0], self.ones_col[0:kr, :],
                                    ab_t[0:kr, w0:w1], start=first, stop=last, skip_group_check=True)
                sum_toks[it] = b.op("tensor", t_sum, waits=[abs_toks[it]], inc="T")

        def v_scalered(E):
            return E.tensor_reduce(out=self.rowbuf[0:1, 0:1], in_=self.ps7[0:1, 0:512],
                                   op=ALU.add, axis=AX.X)
        vsr_ = b.op("vector", v_scalered, waits=[sum_toks[-1]], inc="V")

        def v_scale(E, scale_mode=scale_mode):
            r = self.rowbuf
            if scale_mode == "l0":
                return E.tensor_scalar(out=r[0:1, 1:2], in0=r[0:1, 0:1], scalar1=1.0 / nelem,
                                       scalar2=1e-8, op0=ALU.mult, op1=ALU.add)
            return E.tensor_scalar(out=r[0:1, 1:2], in0=r[0:1, 0:1], scalar1=1.0 / nelem,
                                   scalar2=1e-5, op0=ALU.mult, op1=ALU.max)
        vs0 = b.op("vector", v_scale, waits=[], fwaits=[vsr_], inc="V")

        def v_scale2(E):
            r = self.rowbuf
            return E.reciprocal(out=r[0:1, 2:3], in_=r[0:1, 1:2])
        vs = b.op("vector", v_scale2, waits=[], fwaits=[vs0], inc="V")

        def t_bc(E):
            return E.matmul(self.ps7[0:P, 0:2], self.ones_row_f[:], self.rowbuf[0:1, 1:3],
                            start=True, stop=True, skip_group_check=True)
        tb = b.op("tensor", t_bc, waits=[vs], inc="T")

        def v_bc(E, idx=idx):
            E.tensor_copy(out=self.scal[:, idx:idx + 1], in_=self.ps7[0:P, 0:1])
            return E.tensor_copy(out=self.scal[:, idx + 8:idx + 9], in_=self.ps7[0:P, 1:2])
        vbc = b.op("vector", v_bc, waits=[tb], inc="V")
        q_toks = [None] * n
        st_toks2 = [None] * n
        for it, (iap, oap, kr, w) in enumerate(units):
            wts = [q_toks[it - 2]] if it >= 2 else [vbc]
            b.dma("sync", stg[it % 2][0:kr, 0:w], iap, f"{ld}{it % 2}", waits=wts)
            lt = b.tok(f"{ld}{it % 2}")

            def s_q(E, it=it, kr=kr, w=w, idx=idx):
                E.activation(out=rt_f[0:kr, 0:w], in_=stg[it % 2][0:kr, 0:w], func=AF.Identity,
                             scale=self.scal[0:kr, idx + 8:idx + 9], bias=self.consts[0:kr, 0:1])
                return E.activation(out=qb_t[it % 2][0:kr, 0:w], in_=rt_f[0:kr, 0:w], func=AF.Sign,
                                    bias=self.consts[0:kr, 1:2])
            sq_w = [lt]
            if it >= 2:
                sq_w.append(st_toks2[it - 2])
            q_toks[it] = b.op("scalar", s_q, waits=sq_w, fwaits=[vbc] if it == 0 else [], inc="S")
            st_toks2[it] = b.dma("gpsimd", oap, qb_t[it % 2][0:kr, 0:w], f"{st}{it % 2}", waits=[q_toks[it]])
        return [b.tok(f"{st}0"), b.tok(f"{st}1")]

    def prequant_all(self):
        i, h = self.i, self.h
        self.pq = {}
        u = [(i["wu_t"][k], h["q_wu"][k], P, T) for k in range(NIC0)]
        self.pq["wu"] = self.prequant_one(0, u, "l0")
        u = [(i["wg_t"][k], h["q_wg"][k], P, T) for k in range(NIC0)]
        self.pq["wg"] = self.prequant_one(1, u, "l0", prev=self.pq["wu"])
        u = [(i["wd_t"][k][:, q * 2048:(q + 1) * 2048], h["q_wd"][k][:, q * 2048:(q + 1) * 2048], P, 2048)
             for k in range(NDC) for q in range(4)]
        self.pq["wd"] = self.prequant_one(2, u, "l0", prev=self.pq["wg"])
        def f1units(nat):
            us = []
            for k in range(NDC):
                for isl in range(13):
                    a, c = isl * 512, min(I1, isl * 512 + 512)
                    us.append((nat[k * P:(k + 1) * P, a:c], None, P, c - a, (k, isl)))
            return us
        u = [(i["f1g_nat"][k * P:(k + 1) * P, isl * 512:isl * 512 + 512],
              h["q_f1g"][isl][:, k * 512:(k + 1) * 512], P, 512)
             for k in range(NDC) for isl in range(13)]
        self.pq["f1g"] = self.prequant_one(3, u, "l1", prev=self.pq["wd"], nelem=T * I1)
        u = [(i["f1u_nat"][k * P:(k + 1) * P, isl * 512:isl * 512 + 512],
              h["q_f1u"][isl][:, k * 512:(k + 1) * 512], P, 512)
             for k in range(NDC) for isl in range(13)]
        self.pq["f1u"] = self.prequant_one(4, u, "l1", prev=self.pq["f1g"], nelem=T * I1)
        u = [(i["f1d_nat"][k * P:(k + 1) * P, :], h["q_f1d"][k * P:(k + 1) * P, :], P, T)
             for k in range(NIC1)]
        self.pq["f1d"] = self.prequant_one(5, u, "l1", prev=self.pq["f1u"], nelem=I1 * T)
        b = self.b

        def v_comb(E):
            E.tensor_mul(self.scal[:, 6:7], self.scal[:, 2:3], self.scal[:, 1:2])
            return E.tensor_mul(self.scal[:, 7:8], self.scal[:, 5:6], self.scal[:, 4:5])
        self.scal_done = b.op("vector", v_comb, waits=[], inc="V")
        stg = [self.af(21, 2048), self.af(29, 2048)]
        obs = [self.ab(37, 2048), self.ab(49, 2048)]
        ct = [None, None]
        for eo in range(NDC):
            wts = [ct[eo % 2]] if eo >= 2 else list(self.pq["f1d"])
            b.dma("sync", stg[eo % 2][:], i["wo_t"][eo], f"d_wopc{eo % 2}", waits=wts)
            lt = b.tok(f"d_wopc{eo % 2}")

            def s_c(E, eo=eo):
                return E.copy(out=obs[eo % 2][:], in_=stg[eo % 2][:])
            sc = b.op("scalar", s_c, waits=[lt] + ([ct[eo % 2]] if eo >= 2 else []), inc="S")
            tk = b.dma("gpsimd", h["wo_b"][eo], obs[eo % 2][:], f"d_wopco{eo % 2}", waits=[sc])
            ct[eo % 2] = tk
        self.pq["wo"] = [b.tok("d_wopco0"), b.tok("d_wopco1")]

    def mamba(self):
        b, i = self.b, self.i
        xb = lambda dc: self.BIG[:, dc * 512:(dc + 1) * 512]
        dl = lambda dc: self.BIG[:, 8192 + dc * 512:8192 + (dc + 1) * 512]
        ub = lambda dc: self.BIG[:, 16384 + dc * 512:16384 + (dc + 1) * 512]
        yv = lambda dc: self.af(2 * dc, 512)
        a_t = [self.af(32, 512), self.af(34, 512)]
        b_t = [self.af(36, 512), self.af(38, 512)]
        h_t = [self.af(40, 512), self.af(42, 512)]
        stg = [self.af(44, 512), self.af(46, 512)]
        xpv = self.af(48, 512)[0:96, :]
        dpb = self.ab(50, 512)[0:64, :]
        bcst = self.ab(51, 512)
        brow = self.ab(52, 512)[0:1, :]
        crow = self.ab(52, 1024)[0:1, 512:1024]
        cast_toks = [None] * NDC
        xp_evac = None
        pq_all = [t for v in self.pq.values() for t in (v if isinstance(v, list) else [v])] + (
            [self.scal_done] if getattr(self, "scal_done", None) else [])
        for c in range(4):
            cs = slice(c * 512, (c + 1) * 512)
            for dc in range(NDC):
                wts = [cast_toks[dc - 2]] if dc >= 2 else (pq_all if c == 0 else [b.tok("V")])
                b.dma("sync", stg[dc % 2], i["xT"][dc * P:(dc + 1) * P, cs], f"d_x{dc % 2}", waits=wts)
                lt = b.tok(f"d_x{dc % 2}")

                def v_cast(E, dc=dc):
                    return E.tensor_copy(out=xb(dc), in_=stg[dc % 2])
                cast_toks[dc] = b.op("vector", v_cast, waits=[lt], inc="V")
            for dc in range(NDC):
                def t_xp(E, dc=dc):
                    return E.matmul(self.ps[2][0:96, :], self.wxb[:, dc * 96:(dc + 1) * 96],
                                    xb(dc), start=(dc == 0), stop=(dc == NDC - 1), skip_group_check=True)
                w = [cast_toks[NDC - 1]] if dc == 0 else []
                if dc == 0 and xp_evac is not None:
                    w.append(xp_evac)
                xpt = b.op("tensor", t_xp, waits=w, inc="T" if dc == NDC - 1 else None)

            def s_xp(E):
                return E.copy(out=xpv, in_=self.ps[2][0:96, :])
            xp_evac = b.op("scalar", s_xp, waits=[xpt], inc="S")

            def v_dp(E):
                E.tensor_copy(out=dpb, in_=xpv[0:64, :])
                return E.tensor_copy(out=bcst[64:96, :], in_=xpv[64:96, :])
            dpt = b.op("vector", v_dp, waits=[xp_evac], inc="V")
            sp_toks = [None] * NDC
            vsp_toks = [None] * NDC
            for dc in range(NDC):
                def t_d(E, dc=dc):
                    return E.matmul(self.ps[3][:, :], self.wdtb[:, dc * P:(dc + 1) * P], dpb,
                                    start=True, stop=True, skip_group_check=True)
                w = [dpt] if dc == 0 else [sp_toks[dc - 1]]
                dt_ = b.op("tensor", t_d, waits=w, inc="T")
                t1, t2 = a_t[0], a_t[1]

                def s_sp(E, dc=dc):
                    bias = self.bdt[:, dc:dc + 1]
                    E.activation(out=t1[:], in_=self.ps[3][:, :], func=AF.Relu, bias=bias)
                    E.activation(out=t2[:], in_=self.ps[3][:, :], func=AF.Abs, bias=bias)
                    E.activation(out=t2[:], in_=t2[:], func=AF.Exp, scale=-1.0)
                    return E.activation(out=t2[:], in_=t2[:], func=AF.Ln, bias=self.consts[:, 2:3])
                sp_toks[dc] = b.op("scalar", s_sp, waits=[dt_, vsp_toks[dc - 1] if dc >= 1 else None], inc="S")

                def v_sp(E, dc=dc):
                    E.tensor_add(dl(dc), t1[:], t2[:])
                    return E.tensor_mul(ub(dc), dl(dc), xb(dc))
                vsp_toks[dc] = b.op("vector", v_sp, waits=[sp_toks[dc]], inc="V")
            dready = b.tok("V")
            scan_toks = {}
            bc_read_toks = {}
            for n in range(16):
                def t_ext(E, n=n):
                    E.matmul(self.ps7[0:1, 0:512], self.sel32[64:96, n:n + 1], bcst[64:96, :],
                             start=True, stop=True, skip_group_check=True)
                    return E.matmul(self.ps45[0:1, 0:512], self.sel32[64:96, 16 + n:17 + n],
                                    bcst[64:96, :], start=True, stop=True, skip_group_check=True)
                wx_ = [dready] if n == 0 else [b.tok("S")]
                ext = b.op("tensor", t_ext, waits=wx_, inc="T")

                def s_extev(E):
                    E.copy(out=brow, in_=self.ps7[0:1, 0:512])
                    return E.copy(out=crow, in_=self.ps45[0:1, 0:512])
                sext = b.op("scalar", s_extev, waits=[ext], inc="S")

                def t_bc(E):
                    E.matmul(self.ps[0][:, :], self.ones_row[:], brow,
                             start=True, stop=True, skip_group_check=True)
                    return E.matmul(self.ps[1][:, :], self.ones_row[:], crow,
                                    start=True, stop=True, skip_group_check=True)
                w = [sext] if n == 0 else [sext, scan_toks[(n - 1, NDC - 1)]]
                bct = b.op("tensor", t_bc, waits=w, inc="T")
                for dc in range(NDC):
                    col = dc * 16 + n
                    at = a_t[dc % 2]
                    bt = b_t[dc % 2]
                    ht = h_t[dc % 2]
                    prev = scan_toks.get((n, dc - 2)) or scan_toks.get((n - 1, NDC - 2 + dc)) or dready

                    def s_a(E, dc=dc, col=col, at=at):
                        return E.activation(out=at[:], in_=dl(dc), func=AF.Exp,
                                            scale=self.acols[:, col:col + 1])
                    sa = b.op("scalar", s_a, waits=[prev], inc="S")

                    def v_scan(E, dc=dc, col=col, at=at, bt=bt, ht=ht, n=n):
                        E.tensor_tensor(out=bt[:], in0=ub(dc), in1=self.ps[0][:, :], op=ALU.mult)
                        E.tensor_tensor_scan(out=ht[:], data0=at[:], data1=bt[:],
                                             initial=self.state[:, col:col + 1],
                                             op0=ALU.mult, op1=ALU.add)
                        if n == 0:
                            E.tensor_tensor(out=yv(dc), in0=ht[:], in1=self.ps[1][:, :], op=ALU.mult)
                        else:
                            E.tensor_tensor(out=bt[:], in0=ht[:], in1=self.ps[1][:, :], op=ALU.mult)
                            E.tensor_add(yv(dc), yv(dc), bt[:])
                        return E.tensor_copy(out=self.state[:, col:col + 1], in_=ht[:, 511:512])
                    scan_toks[(n, dc)] = b.op("vector", v_scan, waits=[sa, bct], inc="V")
            if self.upto in ("mdbg", "mdbg2") and c == self.dbg_c:
                ob = self.af(56, 512)
                for dc in range(NDC):
                    def v_d(E, dc=dc):
                        E.tensor_copy(out=ob[:], in_=xb(dc))
                        return ins_ret(E, ob)
                    def ins_ret(E, ob):
                        return E.engine_nop()
                    def v_d1(E, dc=dc):
                        return E.tensor_copy(out=ob[:], in_=xb(dc))
                    vt = b.op("vector", v_d1, waits=[b.tok("V"), b.tok("d_out")], inc="V")
                    b.dma("gpsimd", self.out[dc * P:(dc + 1) * P, 0:512], ob[:], "d_out", waits=[vt])
                    def v_d2(E, dc=dc):
                        return E.tensor_copy(out=ob[:], in_=dl(dc))
                    vt = b.op("vector", v_d2, waits=[b.tok("d_out")], inc="V")
                    b.dma("gpsimd", self.out[dc * P:(dc + 1) * P, 512:1024], ob[:], "d_out", waits=[vt])
                    def v_d3(E, dc=dc):
                        return E.tensor_copy(out=ob[:], in_=yv(dc))
                    vt = b.op("vector", v_d3, waits=[b.tok("d_out")], inc="V")
                    b.dma("gpsimd", self.out[dc * P:(dc + 1) * P, 1024:1536], ob[:], "d_out", waits=[vt])
                if self.upto == "mdbg":
                    def v_d4(E):
                        return E.tensor_copy(out=ob[:, 0:256], in_=self.state[:])
                    vt = b.op("vector", v_d4, waits=[b.tok("d_out")], inc="V")
                    b.dma("gpsimd", self.out[0:P, 1536:1792], ob[:, 0:256], "d_out", waits=[vt])
                else:
                    def v_d6(E):
                        return E.tensor_copy(out=ob[:, 0:256], in_=self.state[:])
                    vt = b.op("vector", v_d6, waits=[b.tok("d_out")], inc="V")
                    b.dma("gpsimd", self.out[0:P, 1536:1792], ob[:, 0:256], "d_out", waits=[vt])
                    for dc in range(NDC):
                        def v_d5(E, dc=dc):
                            return E.tensor_copy(out=ob[:], in_=self.STRM[:, dc * T:dc * T + 512])
                        vt = b.op("vector", v_d5, waits=[b.tok("d_out")], inc="V")
                        b.dma("gpsimd", self.out[dc * P:(dc + 1) * P, 1536:2048] if False else
                              self.out[256 + dc * P:256 + dc * P + P, 0:1] if False else
                              self.out[dc * P:(dc + 1) * P, 1792:2048], ob[:, 0:256], "d_out", waits=[vt])
                self.finish()
                return
            for dc in range(NDC):
                def v_pre(E, dc=dc):
                    return E.scalar_tensor_tensor(out=dl(dc), in0=xb(dc), scalar=self.d1[:, dc:dc + 1],
                                                  in1=yv(dc), op0=ALU.mult, op1=ALU.add)
                b.op("vector", v_pre, waits=[], inc="V")
            pret = b.tok("V")
            self.ln_featmajor(dl, lambda dc: self.STRM[:, dc * T + c * 512:dc * T + (c + 1) * 512], pret)
            if self.upto == "mdbg3" and c == 0:
                ob = self.af(32, 512)
                for dc in range(NDC):
                    def v_d7(E, dc=dc):
                        return E.tensor_copy(out=ob[:], in_=self.STRM[:, dc * T:dc * T + 512])
                    vt = b.op("vector", v_d7, waits=[b.tok("V"), b.tok("S"), b.tok("T"), b.tok("d_out")], inc="V")
                    b.dma("gpsimd", self.out[dc * P:(dc + 1) * P, 0:512], ob[:], "d_out", waits=[vt])
                self.finish()
                return

    def ln_featmajor(self, pre, outv, pret, base=52):
        b = self.b
        mrow = self.af(base, 512)[0:1, :]
        vrow = self.af(base + 2, 512)[0:1, :]
        trow = self.af(base + 4, 512)[0:1, :]
        sq = [self.ab(base + 6, 512), self.ab(base + 7, 512)]
        sq_toks = [None] * NDC
        st_toks = [None] * NDC
        pre_aps = [pre(dc) for dc in range(NDC)]
        out_aps = [outv(dc) for dc in range(NDC)]
        for dc in range(NDC):
            def s_sq(E, dc=dc, pa=pre_aps[dc]):
                return E.activation(out=sq[dc % 2][:], in_=pa, func=AF.Square)
            sq_toks[dc] = b.op("scalar", s_sq, waits=[pret, st_toks[dc - 2] if dc >= 2 else None], inc="S")

            def t_st(E, dc=dc, pa=pre_aps[dc]):
                E.matmul(self.ps45[0:1, 0:512], self.ones_col[:], pa,
                         start=(dc == 0), stop=(dc == NDC - 1), skip_group_check=True)
                return E.matmul(self.ps45[0:1, 512:1024], self.ones_col[:], sq[dc % 2][:],
                                start=(dc == 0), stop=(dc == NDC - 1), skip_group_check=True)
            w_st = [sq_toks[dc], pret]
            if dc == 0 and getattr(self, "ln_prev_app", None) is not None:
                w_st.append(self.ln_prev_app)
            st_toks[dc] = b.op("tensor", t_st, waits=w_st, inc="T")

        def v_mv(E):
            E.tensor_scalar(out=mrow, in0=self.ps45[0:1, 0:512], scalar1=1.0 / T, scalar2=None, op0=ALU.mult)
            E.tensor_scalar(out=vrow, in0=self.ps45[0:1, 512:1024], scalar1=1.0 / T, scalar2=None, op0=ALU.mult)
            E.tensor_mul(trow, mrow, mrow)
            return E.tensor_sub(vrow, vrow, trow)
        vmv = b.op("vector", v_mv, waits=[st_toks[-1]], inc="V")

        def s_inv(E):
            E.activation(out=vrow, in_=vrow, func=AF.Ln, bias=self.consts[0:1, 3:4])
            return E.activation(out=vrow, in_=vrow, func=AF.Exp, scale=-0.5)
        si = b.op("scalar", s_inv, waits=[vmv], inc="S")

        def v_rows(E):
            return E.tensor_mul(trow, mrow, vrow)
        vr = b.op("vector", v_rows, waits=[si], inc="V")

        def t_bc(E):
            E.matmul(self.ps45[0:P, 0:512], self.ones_row_f[:], vrow, start=True, stop=True,
                     skip_group_check=True)
            return E.matmul(self.ps45[0:P, 512:1024], self.ones_row_f[:], trow, start=True, stop=True,
                            skip_group_check=True)
        tbc = b.op("tensor", t_bc, waits=[vr], inc="T")
        for dc in range(NDC):
            def v_app(E, dc=dc, pa=pre_aps[dc], oa=out_aps[dc]):
                tmp = self.af(base if dc % 2 == 0 else base + 2, 512)
                E.tensor_tensor(out=tmp, in0=pa, in1=self.ps45[0:P, 0:512], op=ALU.mult)
                return E.tensor_tensor(out=oa, in0=tmp, in1=self.ps45[0:P, 512:1024], op=ALU.subtract)
            b.op("vector", v_app, waits=[tbc], inc="V")
        self.ln_prev_app = b.tok("V")
        return self.ln_prev_app

    def ffn0(self):
        b, h = self.b, self.h
        hid = lambda ic: self.BIG[:, ic * 512:(ic + 1) * 512]
        ug_slot = lambda k: self.ab(4 * k, T)
        wd_slot = lambda k: self.ab(16 + 16 * k, I0)
        su = self.af(48, 512)
        tt_ = self.af(50, 512)
        pre = lambda dc: self.ab(0, 8192)[:, dc * 512:(dc + 1) * 512]
        pqw = self.pq["wu"] + self.pq["wg"] + self.pq["wd"] + [self.scal_done]
        for tb in range(4):
            hid_toks = [None] * NIC0
            ug_l_toks = [None] * NIC0
            for ic in range(NIC0):
                sl = ic % 2
                wts = [hid_toks[ic - 2]] if ic >= 2 else (pqw + [b.tok("V")] if tb == 0 else [b.tok("V")])
                b.dma("sync", ug_slot(sl), h["q_wu"][ic], f"dfu{sl}", waits=wts)
                b.dma("sync", ug_slot(2 + sl), h["q_wg"][ic], f"dfg{sl}", waits=wts)
                lu, lg = b.tok(f"dfu{sl}"), b.tok(f"dfg{sl}")
                ug_l_toks[ic] = (lu, lg)
                pu, pg = self.ps[sl], self.ps[2 + sl]
                for dc in range(NDC):
                    def t_ug(E, dc=dc, sl=sl, tb=tb):
                        x = self.STRM[:, dc * T + tb * 512:dc * T + (tb + 1) * 512]
                        E.matmul(self.ps[sl][:, :], ug_slot(sl)[:, dc * P:(dc + 1) * P], x,
                                 start=(dc == 0), stop=(dc == NDC - 1), skip_group_check=True)
                        return E.matmul(self.ps[2 + sl][:, :], ug_slot(2 + sl)[:, dc * P:(dc + 1) * P], x,
                                        start=(dc == 0), stop=(dc == NDC - 1), skip_group_check=True)
                    w = [lu, lg] if dc == 0 else []
                    if dc == 0 and ic >= 2:
                        w.append(hid_toks[ic - 2])
                    mt = b.op("tensor", t_ug, waits=w, inc="T" if dc == NDC - 1 else None)

                def s_silu(E, sl=sl):
                    return E.activation(out=su[:], in_=self.ps[sl][:, :], func=AF.Silu,
                                        scale=self.scal[:, 0:1])
                st = b.op("scalar", s_silu, waits=[mt], inc="S")

                def v_hid(E, ic=ic, sl=sl):
                    return E.tensor_tensor(out=hid(ic), in0=su[:], in1=self.ps[2 + sl][:, :], op=ALU.mult)
                hid_toks[ic] = b.op("vector", v_hid, waits=[st, mt], inc="V")
            dn_toks = [None] * NDC
            for dc in range(NDC):
                sl = dc % 2
                wts = [dn_toks[dc - 2]] if dc >= 2 else [hid_toks[-1]]
                b.dma("sync", wd_slot(sl), h["q_wd"][dc], f"dfd{sl}", waits=wts)
                ldd = b.tok(f"dfd{sl}")
                pd = self.ps45[:, sl * 512:sl * 512 + 512]
                for ic in range(NIC0):
                    def t_dn(E, ic=ic, sl=sl):
                        return E.matmul(self.ps45[:, sl * 512:sl * 512 + 512],
                                        wd_slot(sl)[:, ic * P:(ic + 1) * P], hid(ic),
                                        start=(ic == 0), stop=(ic == NIC0 - 1), skip_group_check=True)
                    w = [ldd, hid_toks[-1]] if ic == 0 else []
                    if ic == 0 and dc >= 2:
                        w.append(dn_toks[dc - 2])
                    mt = b.op("tensor", t_dn, waits=w, inc="T" if ic == NIC0 - 1 else None)

                def s_dn(E, sl=sl):
                    return E.activation(out=tt_[:], in_=self.ps45[:, sl * 512:sl * 512 + 512],
                                        func=AF.Identity, scale=self.scal[:, 6:7])
                sdt = b.op("scalar", s_dn, waits=[mt], inc="S")

                def v_dn(E, dc=dc, tb=tb):
                    x = self.STRM[:, dc * T + tb * 512:dc * T + (tb + 1) * 512]
                    return E.tensor_tensor(out=pre(dc), in0=tt_[:], in1=x, op=ALU.add)
                dn_toks[dc] = b.op("vector", v_dn, waits=[sdt], inc="V")
            pret = b.tok("V")
            self.ln_featmajor(pre, lambda dc: self.STRM[:, dc * T + tb * 512:dc * T + (tb + 1) * 512], pret)

    def attn(self):
        b, i, h = self.b, self.i, self.h
        K_ = lambda ec: self.BIG[:, ec * T:(ec + 1) * T]
        stg = [self.af(0, 2048), self.af(8, 2048)]
        wkb = self.ab(16, 2048)
        wqb = self.ab(20, 2048)
        vtile = self.ab(24, 2048)
        wvb = lambda dc: self.BIG[:, dc * T:(dc + 1) * T]
        cast_toks = [None] * NDC
        for dc in range(NDC):
            wts = [cast_toks[dc - 2]] if dc >= 2 else [b.tok("V")]
            b.dma("sync", stg[dc % 2], i["wv_nat"][dc * P:(dc + 1) * P, :], f"d_wv{dc % 2}", waits=wts)
            lt = b.tok(f"d_wv{dc % 2}")

            def s_c(E, dc=dc):
                return E.copy(out=wvb(dc), in_=stg[dc % 2])
            cast_toks[dc] = b.op("scalar", s_c, waits=[lt] + ([b.tok("T")] if dc == 0 else []), inc="S")
        wvt = cast_toks[-1]
        ev_toks = {}
        for tt in range(16):
            for es in range(4):
                for dc in range(NDC):
                    def t_v(E, tt=tt, es=es, dc=dc):
                        return E.matmul(self.ps[es % 2][:, :],
                                        self.STRM[:, dc * T + tt * P:dc * T + (tt + 1) * P],
                                        wvb(dc)[:, es * 512:(es + 1) * 512],
                                        start=(dc == 0), stop=(dc == NDC - 1), skip_group_check=True)
                    w = []
                    if dc == 0:
                        w = [wvt]
                        pk = ev_toks.get((tt * 4 + es) - 2)
                        if pk is not None:
                            w.append(pk)
                    mt = b.op("tensor", t_v, waits=w, inc="T" if dc == NDC - 1 else None)

                def v_ev(E, es=es):
                    return E.tensor_copy(out=vtile[:, es * 512:(es + 1) * 512], in_=self.ps[es % 2][:, :])
                wev = [mt] + ([b.tok("d_vsp")] if es == 0 and tt > 0 else [])
                ev_toks[tt * 4 + es] = b.op("vector", v_ev, waits=wev, inc="V")
            b.dma("gpsimd", h["vspill"][tt], vtile[:], "d_vsp", waits=[ev_toks[tt * 4 + 3]])
        vsp_done = b.tok("d_vsp")
        qtile = vtile
        kq_ev = {}
        kq_mm_last = None
        for ec in range(NDC):
            wts = list(kq_ev.get((ec - 1, 3), ())) if ec >= 1 else [vsp_done]
            b.dma("sync", stg[0], i["wk_t"][ec], "d_wk", waits=wts)
            b.dma("sync", stg[1], i["wq_t"][ec], "d_wq", waits=wts)
            l1, l2 = b.tok("d_wk"), b.tok("d_wq")

            def s_ck(E):
                E.copy(out=wkb[:], in_=stg[0])
                return E.copy(out=wqb[:], in_=stg[1])
            ck = b.op("scalar", s_ck, waits=[l1, l2, kq_mm_last], inc="S")
            for ts in range(4):
                for dc in range(NDC):
                    def t_kq(E, ec=ec, ts=ts, dc=dc):
                        x = self.STRM[:, dc * T + ts * 512:dc * T + (ts + 1) * 512]
                        E.matmul(self.ps[ts % 2][:, :], wkb[:, dc * P:(dc + 1) * P], x,
                                 start=(dc == 0), stop=(dc == NDC - 1), skip_group_check=True)
                        return E.matmul(self.ps[2 + ts % 2][:, :], wqb[:, dc * P:(dc + 1) * P], x,
                                        start=(dc == 0), stop=(dc == NDC - 1), skip_group_check=True)
                    w = [ck] if dc == 0 else []
                    if dc == 0:
                        pk = kq_ev.get((ec, ts - 2)) or kq_ev.get((ec - 1, 2 + ts))
                        if pk is not None:
                            w.extend(pk)
                    mt = b.op("tensor", t_kq, waits=w, inc="T" if dc == NDC - 1 else None)
                kq_mm_last = mt

                def v_evk(E, ec=ec, ts=ts):
                    return E.tensor_copy(out=K_(ec)[:, ts * 512:(ts + 1) * 512], in_=self.ps[ts % 2][:, :])
                vk = b.op("vector", v_evk, waits=[mt], inc="V")

                def s_evq(E, ts=ts):
                    return E.copy(out=qtile[:, ts * 512:(ts + 1) * 512], in_=self.ps[2 + ts % 2][:, :])
                wq2 = [mt] + ([b.tok("d_qsp")] if ts == 0 and ec > 0 else [])
                sq_ = b.op("scalar", s_evq, waits=wq2, inc="S")
                kq_ev[(ec, ts)] = (vk, sq_)
            b.dma("gpsimd", h["qspill"][ec], qtile[:], "d_qsp", waits=[kq_ev[(ec, 3)][1], b.tok("V")])
        kq_done = b.tok("d_qsp")
        qt_b = [self.ab(16, 2048), self.ab(20, 2048)]
        P0 = self.ab(24, 2048)
        P1 = self.ab(28, 2048)
        Ptil = lambda hh: self.ab(32 + 4 * hh, 2048)
        vblk = [self.ab(48, 2048), self.ab(52, 2048)]
        ptile = self.ab(56, 512)
        ofin = self.ab(57, 1024)
        mxc = self.af(62, 256)
        OtrT = self.ab(0, 4096)
        OtrT = self.ab(0, 8192)
        wostrip = [self.ab(16, 2048), self.ab(20, 2048)]
        wostrip = [self.ab(32, 2048), self.ab(36, 2048)]
        preLN2 = lambda dc: self.ab(40, 8192)[:, dc * 512:(dc + 1) * 512]
        x2tok = self.ab(0, 2048)
        xqst = self.ab(4, 2048)
        sqf = self.af(28, 512)
        self.attn_bufs = dict(OtrT=OtrT, wostrip=wostrip, preLN2=preLN2, x2tok=x2tok,
                              xqst=xqst, sqf=sqf, mxc=mxc)
        pv_toks = {}
        qld_toks = {}
        exp_toks = {}
        self.vblk_last = {0: None, 1: None}
        self.ps45_free = None
        for ec in range(NDC):
            b.dma("sync", qt_b[0][:, ec * P:(ec + 1) * P], h["qspill"][ec][:, 0:P], "d_qt0",
                  waits=[kq_done] if ec == 0 else [])
        qld_toks[0] = b.tok("d_qt0")
        if self.upto == "attn_kv":
            return
        if self.upto == "attn_probe":
            ob = self.af(56, 2048)
            stg3 = self.ab(60, 2048)
            specs = [(0, h["qspill"][0], "hbm"), (1, h["qspill"][15], "hbm"),
                     (2, h["vspill"][0], "hbm"), (3, h["vspill"][15], "hbm")]
            for kk, src_ap, kind in specs:
                b.dma("sync", stg3, src_ap, "d_pr2", waits=[kq_done, b.tok("d_out")])
                lt = b.tok("d_pr2")
                def v_cp(E):
                    return E.tensor_copy(out=ob[:], in_=stg3)
                vt = b.op("vector", v_cp, waits=[lt, b.tok("d_out")], inc="V")
                b.dma("gpsimd", self.out[kk * 128:(kk + 1) * 128, :], ob[:], "d_out", waits=[vt])
            for kk, ec in [(4, 0), (5, 15)]:
                def v_ck(E, ec=ec):
                    return E.tensor_copy(out=ob[:], in_=K_(ec))
                vt = b.op("vector", v_ck, waits=[b.tok("d_out"), b.tok("V")], inc="V")
                b.dma("gpsimd", self.out[kk * 128:(kk + 1) * 128, :], ob[:], "d_out", waits=[vt])
            return
        ntq = 4 if self.upto in ("attn_ts0", "attn_sc", "attn_pv", "attn_odump") else 16
        for tq in range(ntq):
            qt_cur = qt_b[tq % 2]
            if tq < ntq - 1:
                for ec in range(NDC):
                    w = []
                    if ec == 0 and tq >= 1:
                        w = [exp_toks.get((tq - 1, 7, 1))]
                    b.dma("sync", qt_b[(tq + 1) % 2][:, ec * P:(ec + 1) * P],
                          h["qspill"][ec][:, (tq + 1) * P:(tq + 2) * P], f"d_qt{(tq + 1) % 2}", waits=w)
                qld_toks[tq + 1] = b.tok(f"d_qt{(tq + 1) % 2}")
            qld = qld_toks[tq]
            nvk = tq + 1
            wvalid = nvk * P
            nsl = (wvalid + 511) // 512
            for hg in range(2):
                comb_toks = {}
                for hh in range(4):
                    hd = hg * 4 + hh
                    for j in (0, 1):
                        ec = hd * 2 + j
                        Pj = P0 if j == 0 else P1
                        for sl in range(nsl):
                            w0, w1 = sl * 512, min(wvalid, sl * 512 + 512)

                            def t_sc(E, ec=ec, sl=sl, w0=w0, w1=w1, qt_cur=qt_cur):
                                return E.matmul(self.ps[sl][:, 0:w1 - w0],
                                                qt_cur[:, ec * P:(ec + 1) * P], K_(ec)[:, w0:w1],
                                                start=True, stop=True, skip_group_check=True)
                            w = []
                            if sl == 0:
                                w = [qld, b.tok("S"), b.tok("V")]
                            mt = b.op("tensor", t_sc, waits=w, inc="T" if sl == nsl - 1 else None)

                        def v_mask(E, nsl=nsl, wvalid=wvalid):
                            lsl = nsl - 1
                            off = (wvalid - P) - lsl * 512
                            E.tensor_tensor(out=self.ps[lsl][:, off:off + P],
                                            in0=self.ps[lsl][:, off:off + P],
                                            in1=self.causal[:], op=ALU.add)
                            for sl in range(nsl):
                                w0, w1 = sl * 512, min(wvalid, sl * 512 + 512)
                                ins = E.tensor_reduce(out=mxc[:, sl:sl + 1], in_=self.ps[sl][:, 0:w1 - w0],
                                                      op=ALU.max, axis=AX.X)
                            return ins
                        vm0 = b.op("vector", v_mask, waits=[mt, b.tok("S")], inc="V")

                        def v_maxc(E, nsl=nsl):
                            ins = E.tensor_reduce(out=mxc[:, 5:6], in_=mxc[:, 0:nsl], op=ALU.max, axis=AX.X)
                            return ins
                        vm1 = b.op("vector", v_maxc, waits=[], fwaits=[vm0], inc="V")

                        def v_bias(E):
                            return E.tensor_scalar(out=mxc[:, 4:5], in0=mxc[:, 5:6],
                                                   scalar1=-INV_SQRT_D, scalar2=None, op0=ALU.mult)
                        vx = b.op("vector", v_bias, waits=[], fwaits=[vm1], inc="V")

                        def s_exp(E, Pj=Pj, nsl=nsl, wvalid=wvalid):
                            ins = None
                            for sl in range(nsl):
                                w0, w1 = sl * 512, min(wvalid, sl * 512 + 512)
                                ins = E.activation(out=Pj[:, w0:w1], in_=self.ps[sl][:, 0:w1 - w0],
                                                   func=AF.Exp, scale=INV_SQRT_D,
                                                   bias=mxc[:, 4:5], accum_out=mxc[:, 8 + sl:9 + sl])
                            return ins
                        se = b.op("scalar", s_exp, waits=[vx], fwaits=[vx], inc="S")
                        exp_toks[(tq, hd, j)] = se

                        def v_rs(E, j=j, nsl=nsl):
                            return E.tensor_reduce(out=mxc[:, 12 + j:13 + j], in_=mxc[:, 8:8 + nsl],
                                                   op=ALU.add, axis=AX.X)
                        b.op("vector", v_rs, waits=[se], inc="V")
                    def v_c0a(E):
                        E.reciprocal(out=mxc[:, 16:17], in_=mxc[:, 12:13])
                        return E.reciprocal(out=mxc[:, 17:18], in_=mxc[:, 13:14])
                    c0a = b.op("vector", v_c0a, waits=[], fwaits=[b.tok("V")], inc="V")

                    def v_c0(E):
                        return E.tensor_mul(mxc[:, 18:19], mxc[:, 17:18], self.lam[:, 1:2])
                    c0 = b.op("vector", v_c0, waits=[], fwaits=[c0a], inc="V")

                    def v_c1(E, hh=hh, wvalid=wvalid):
                        E.tensor_scalar(out=Ptil(hh)[:, 0:wvalid], in0=P0[:, 0:wvalid],
                                        scalar1=mxc[:, 16:17], scalar2=None, op0=ALU.mult)
                        return E.scalar_tensor_tensor(out=Ptil(hh)[:, 0:wvalid], in0=P1[:, 0:wvalid],
                                                      scalar=mxc[:, 18:19], in1=Ptil(hh)[:, 0:wvalid],
                                                      op0=ALU.mult, op1=ALU.add)
                    comb_toks[hh] = b.op("vector", v_c1, waits=[b.tok("T")], fwaits=[c0], inc="V")
                combt = comb_toks[3]
                if self.upto == "attn_sc":
                    continue
                for tk in range(nvk):
                    slv = tk % 2
                    w = [combt, self.vblk_last.get(slv), vsp_done]
                    b.dma("sync", vblk[slv][:], h["vspill"][tk], f"d_vb{slv}", waits=w)
                    vld = b.tok(f"d_vb{slv}")
                    for hh in range(4):
                        hd = hg * 4 + hh
                        et = self.transpose_pair(Ptil(hh)[:, tk * P:(tk + 1) * P],
                                                 ptile[:, hh * P:(hh + 1) * P],
                                                 [comb_toks[hh]], "scalar")

                        def t_pv(E, hh=hh, hd=hd, tk=tk, slv=slv, nvk=nvk):
                            return E.matmul(self.ps[hh][:, 0:256],
                                            ptile[:, hh * P:(hh + 1) * P],
                                            vblk[slv][:, hd * 256:(hd + 1) * 256],
                                            start=(tk == 0), stop=(tk == nvk - 1), skip_group_check=True)
                        wpv = [et, vld]
                        if tk == 0:
                            wpv.append(self.ps45_free)
                        pv_toks[(tq, hg, tk)] = b.op("tensor", t_pv, waits=wpv, inc="T")
                    self.vblk_last[slv] = pv_toks[(tq, hg, tk)]
                pvt = pv_toks[(tq, hg, nvk - 1)]
                for hh in range(4):
                    hd = hg * 4 + hh

                    def s_sq(E, hh=hh):
                        return E.activation(out=sqf[:, 0:256], in_=self.ps[hh][:, 0:256],
                                            func=AF.Square)
                    ssq = b.op("scalar", s_sq, waits=[pvt] + ([b.tok("V")] if hh > 0 else []), inc="S")

                    def v_ms(E):
                        return E.tensor_reduce(out=mxc[:, 20:21], in_=sqf[:, 0:256], op=ALU.add, axis=AX.X)
                    vms = b.op("vector", v_ms, waits=[ssq], inc="V")

                    def s_ia(E):
                        return E.activation(out=mxc[:, 22:23], in_=mxc[:, 20:21], func=AF.Ln,
                                            scale=1.0 / 256.0, bias=self.consts[:, 3:4])
                    sia = b.op("scalar", s_ia, waits=[vms], inc="S")

                    def s_ib(E):
                        return E.activation(out=mxc[:, 21:22], in_=mxc[:, 22:23], func=AF.Exp, scale=-0.5)
                    sin = b.op("scalar", s_ib, waits=[], fwaits=[sia], inc="S")

                    def v_o(E, hh=hh):
                        E.tensor_scalar(out=sqf[:, 0:256], in0=self.ps[hh][:, 0:256],
                                        scalar1=mxc[:, 21:22], scalar2=None, op0=ALU.mult)
                        return E.tensor_tensor(out=ofin[:, hh * 256:(hh + 1) * 256], in0=sqf[:, 0:256],
                                               in1=self.g1[:, 0:256], op=ALU.mult)
                    vo = b.op("vector", v_o, waits=[sin], fwaits=[sin], inc="V")
                    if hh == 3:
                        self.ps45_free = vo
                    for half in range(2):
                        e2 = hd * 2 + half
                        fw = [b.tok("d_xq")] if (tq % 4 == 0 and hg == 0 and hh == 0 and half == 0) else []
                        self.transpose_pair(
                            ofin[:, hh * 256 + half * 128:hh * 256 + half * 128 + 128],
                            OtrT[:, e2 * 512 + (tq % 4) * 128:e2 * 512 + (tq % 4) * 128 + 128],
                            [vo], "scalar", fw=fw)
            if tq % 4 == 3 and self.upto not in ("attn_sc", "attn_pv"):
                self.attn_out_ts(tq // 4)
                if getattr(self, "attn_done_early", False):
                    return

    def attn_out_ts(self, s_):
        b, h = self.b, self.h
        A = self.attn_bufs
        OtrT, wostrip, preLN2 = A["OtrT"], A["wostrip"], A["preLN2"]
        x2tok, xqst, sqf, mxc = A["x2tok"], A["xqst"], A["sqf"], A["mxc"]
        otr_done = b.tok("S")
        if self.upto == "attn_odump" and s_ == 0:
            ob2 = self.af(56, 2048)
            for blk in range(4):
                def v_od(E, blk=blk):
                    return E.tensor_copy(out=ob2[:], in_=OtrT[:, blk * 2048:(blk + 1) * 2048])
                vt = b.op("vector", v_od, waits=[otr_done, b.tok("d_out")], inc="V")
                b.dma("gpsimd", self.out[blk * 128:(blk + 1) * 128, :], ob2[:], "d_out", waits=[vt])
            self.finish()
            self.attn_done_early = True
            return
        res_toks = [None] * NDC
        for eo in range(NDC):
            sl = eo % 2
            wts = [res_toks[eo - 2]] if eo >= 2 else (self.pq["wo"] + [otr_done] if s_ == 0 else [otr_done])
            b.dma("sync", wostrip[sl][:], h["wo_b"][eo], f"dwo{sl}", waits=wts)
            lt = b.tok(f"dwo{sl}")
            for e2 in range(NDC):
                def t_wo(E, e2=e2, sl=sl):
                    return E.matmul(self.ps[sl][:, :], wostrip[sl][:, e2 * P:(e2 + 1) * P],
                                    OtrT[:, e2 * 512:(e2 + 1) * 512],
                                    start=(e2 == 0), stop=(e2 == NDC - 1), skip_group_check=True)
                w = [lt, otr_done] if e2 == 0 else []
                if e2 == 0 and eo >= 2:
                    w.append(res_toks[eo - 2])
                mt = b.op("tensor", t_wo, waits=w, inc="T" if e2 == NDC - 1 else None)

            def v_res(E, eo=eo, s_=s_, sl=sl):
                x = self.STRM[:, eo * T + s_ * 512:eo * T + (s_ + 1) * 512]
                return E.tensor_tensor(out=preLN2(eo), in0=self.ps[sl][:, :], in1=x, op=ALU.add)
            res_toks[eo] = b.op("vector", v_res, waits=[mt], inc="V")
        lnt = self.ln_featmajor(preLN2, preLN2, b.tok("V"), base=24)
        for l in range(4):
            ttg = 4 * s_ + l
            x2t = None
            for dc in range(NDC):
                fw0 = [b.tok("d_xq"), b.tok("V")] if dc == 0 else []
                x2t = self.transpose_pair(preLN2(dc)[:, l * P:(l + 1) * P],
                                          x2tok[:, dc * P:(dc + 1) * P], [lnt], "scalar", fw=fw0)

            def v_sra(E):
                E.tensor_reduce(out=mxc[:, 32:33], in_=x2tok[:], op=ALU.max, axis=AX.X)
                return E.tensor_reduce(out=mxc[:, 33:34], in_=x2tok[:], op=ALU.min, axis=AX.X)
            vsra = b.op("vector", v_sra, waits=[x2t], inc="V")

            def v_srb0(E):
                return E.tensor_scalar(out=mxc[:, 33:34], in0=mxc[:, 33:34], scalar1=-1.0,
                                       scalar2=None, op0=ALU.mult)
            vsrb0 = b.op("vector", v_srb0, waits=[], fwaits=[vsra], inc="V")

            def v_srb(E, ttg=ttg):
                E.tensor_scalar(out=mxc[:, 32:33], in0=mxc[:, 32:33], scalar1=mxc[:, 33:34],
                                scalar2=1e-8, op0=ALU.max, op1=ALU.max)
                return E.tensor_scalar(out=self.srinv[:, ttg:ttg + 1], in0=mxc[:, 32:33],
                                       scalar1=1.0 / 127.0, scalar2=None, op0=ALU.mult)
            vsrb = b.op("vector", v_srb, waits=[], fwaits=[vsrb0], inc="V")

            def v_src(E, ttg=ttg):
                return E.reciprocal(out=mxc[:, 34:35], in_=self.srinv[:, ttg:ttg + 1])
            vsr = b.op("vector", v_src, waits=[], fwaits=[vsrb], inc="V")

            def s_rq(E):
                ins = None
                for q in range(4):
                    E.activation(out=sqf[:, 0:512], in_=x2tok[:, q * 512:(q + 1) * 512], func=AF.Identity,
                                 scale=mxc[:, 34:35], bias=self.consts[:, 0:1])
                    ins = E.activation(out=xqst[:, q * 512:(q + 1) * 512], in_=sqf[:, 0:512],
                                       func=AF.Identity, bias=self.consts[:, 1:2])
                return ins
            srq = b.op("scalar", s_rq, waits=[vsr], fwaits=[vsr], inc="S")

            def v_pack(E, s_=s_, l=l):
                ins = None
                for j in range(4):
                    ins = E.tensor_copy(
                        out=self.STRM[:, (4 * j + l) * T + s_ * 512:(4 * j + l) * T + (s_ + 1) * 512],
                        in_=x2tok[:, j * 512:(j + 1) * 512])
                return ins
            b.op("vector", v_pack, waits=[x2t], inc="V")
            evq = None
            for dc in range(NDC):
                fw = [b.tok("d_xq"), b.tok("V")] if dc == 0 else []
                evq = self.transpose_pair(xqst[:, dc * P:(dc + 1) * P],
                                          x2tok[:, dc * P:(dc + 1) * P], [srq], "vector", fw=fw)
            for dc in range(NDC):
                b.dma("gpsimd", h["xqspill"][dc][:, ttg * P:(ttg + 1) * P],
                      x2tok[:, dc * P:(dc + 1) * P], "d_xq", waits=[evq])

    def ffn1(self):
        b, h = self.b, self.h
        hidv = lambda tl, c0, c1: self.BIG[:, tl * 8192 + c0:tl * 8192 + c1]
        xqb = lambda dc: self.ab(0, 8192)[:, dc * 512:(dc + 1) * 512]
        gb = lambda dc: self.ab(16, 8192)[:, dc * 512:(dc + 1) * 512]
        ub_ = lambda dc: self.ab(32, 8192)[:, dc * 512:(dc + 1) * 512]
        su = self.af(48, 512)
        rmx = self.rowbuf
        pqw = self.pq["f1g"] + self.pq["f1u"] + self.pq["f1d"]
        xq_done = b.tok("d_xq")
        out_toks = []
        for tb in range(4):
            wts = [xq_done] + (pqw if tb == 0 else [b.tok("T")])
            for dc in range(NDC):
                b.dma("sync", xqb(dc), h["xqspill"][dc][:, tb * 512:(tb + 1) * 512], "d_xqi",
                      waits=wts if dc == 0 else [])
            xql = b.tok("d_xqi")

            def v_fs(E, tb=tb):
                ins = None
                for tl in range(4):
                    E.memset(rmx[:, tl:tl + 1], 0.0)
                    E.memset(rmx[:, 4 + tl:5 + tl], 0.0)
                    ins = E.tensor_tensor(out=self.fscal[:, tl * 4:tl * 4 + 1],
                                          in0=self.srinv[:, tb * 4 + tl:tb * 4 + tl + 1],
                                          in1=self.scal[:, 3:4], op=ALU.mult)
                return ins
            fst = b.op("vector", v_fs, waits=[self.scal_done] + ([b.tok("S")] if tb > 0 else []), inc="V")
            h_toks = {}
            gu_mm_last = None
            for isl in range(13):
                w0 = isl * 512
                w1 = w0 + 512
                ww = w1 - w0
                gq_toks, uq_toks = [], []
                for qq in range(4):
                    wts = []
                    if qq == 0:
                        if isl >= 1:
                            wts = [gu_mm_last]
                        elif tb > 0:
                            wts = [b.tok("T"), b.tok("V")]
                        else:
                            wts = [xq_done]
                    b.dma("sync", self.ab(16, 8192)[:, qq * 2048:(qq + 1) * 2048],
                          h["q_f1g"][isl][:, qq * 2048:(qq + 1) * 2048], f"d_f1g{qq}", waits=wts)
                    gq_toks.append(b.tok(f"d_f1g{qq}"))
                    b.dma("sync", self.ab(32, 8192)[:, qq * 2048:(qq + 1) * 2048],
                          h["q_f1u"][isl][:, qq * 2048:(qq + 1) * 2048], f"d_f1u{qq}", waits=[])
                    uq_toks.append(b.tok(f"d_f1u{qq}"))
                for tl in range(4):
                    for qq in range(4):
                        def t_gu(E, tl=tl, ww=ww, qq=qq):
                            ins = None
                            for dc in range(qq * 4, qq * 4 + 4):
                                xt_ = xqb(dc)[:, tl * P:(tl + 1) * P]
                                E.matmul(self.ps[tl % 2][:, 0:ww], xt_, gb(dc)[:, 0:ww],
                                         start=(dc == 0), stop=(dc == NDC - 1), skip_group_check=True)
                                ins = E.matmul(self.ps[2 + tl % 2][:, 0:ww], xt_, ub_(dc)[:, 0:ww],
                                               start=(dc == 0), stop=(dc == NDC - 1), skip_group_check=True)
                            return ins
                        w = [gq_toks[qq], uq_toks[qq]]
                        if qq == 0:
                            w += [xql, fst]
                            pk = h_toks.get((isl, tl - 2)) or h_toks.get((isl - 1, 2 + tl))
                            if pk is not None:
                                w.append(pk)
                        mt = b.op("tensor", t_gu, waits=w, inc="T")
                    gu_mm_last = mt

                    def s_si(E, tl=tl, ww=ww):
                        return E.activation(out=su[:, 0:ww], in_=self.ps[tl % 2][:, 0:ww], func=AF.Silu,
                                            scale=self.fscal[:, tl * 4:tl * 4 + 1])
                    ssi = b.op("scalar", s_si, waits=[mt], fwaits=[fst], inc="S")

                    def v_h(E, tl=tl, w0=w0, ww=ww):
                        E.tensor_tensor(out=hidv(tl, w0, w0 + ww), in0=su[:, 0:ww],
                                        in1=self.ps[2 + tl % 2][:, 0:ww], op=ALU.mult)
                        E.tensor_reduce(out=rmx[:, 8:9], in_=hidv(tl, w0, w0 + ww), op=ALU.max, axis=AX.X)
                        E.tensor_reduce(out=rmx[:, 9:10], in_=hidv(tl, w0, w0 + ww), op=ALU.min, axis=AX.X)
                        E.tensor_tensor(out=rmx[:, tl:tl + 1], in0=rmx[:, tl:tl + 1], in1=rmx[:, 8:9], op=ALU.max)
                        E.tensor_scalar(out=rmx[:, 9:10], in0=rmx[:, 9:10], scalar1=-1.0, scalar2=None, op0=ALU.mult)
                        E.engine_nop()
                        return E.tensor_tensor(out=rmx[:, 4 + tl:5 + tl], in0=rmx[:, 4 + tl:5 + tl],
                                               in1=rmx[:, 9:10], op=ALU.max)
                    h_toks[(isl, tl)] = b.op("vector", v_h, waits=[ssi, mt], inc="V")
            hq_toks = []
            for tl in range(4):
                ttg = tb * 4 + tl

                def v_sha(E, tl=tl):
                    return E.tensor_scalar(out=rmx[:, 8:9], in0=rmx[:, tl:tl + 1],
                                           scalar1=rmx[:, 4 + tl:5 + tl], scalar2=1e-5,
                                           op0=ALU.max, op1=ALU.max)
                vsha = b.op("vector", v_sha, waits=[h_toks[(12, 3)]], fwaits=[b.tok("V")], inc="V")

                def v_shb(E, tl=tl, ttg=ttg):
                    E.reciprocal(out=rmx[:, 10:11], in_=rmx[:, 8:9])
                    return E.tensor_tensor(out=rmx[:, 11:12], in0=self.srinv[:, ttg:ttg + 1],
                                           in1=rmx[:, 8:9], op=ALU.mult)
                vshb = b.op("vector", v_shb, waits=[], fwaits=[vsha], inc="V")

                def v_shc(E, tl=tl):
                    E.tensor_scalar(out=self.fscal[:, tl * 4 + 2:tl * 4 + 3], in0=rmx[:, 10:11],
                                    scalar1=127.0, scalar2=None, op0=ALU.mult)
                    return E.tensor_scalar(out=self.fscal[:, tl * 4 + 1:tl * 4 + 2], in0=rmx[:, 11:12],
                                           scalar1=self.scal[:, 7:8], scalar2=1.0 / 127.0,
                                           op0=ALU.mult, op1=ALU.mult)
                vsh = b.op("vector", v_shc, waits=[], fwaits=[vshb], inc="V")

                def s_hq(E, tl=tl):
                    ins = None
                    for isl in range(13):
                        w0, w1 = isl * 512, isl * 512 + 512
                        E.activation(out=su[:, 0:w1 - w0], in_=hidv(tl, w0, w1), func=AF.Identity,
                                     scale=self.fscal[:, tl * 4 + 2:tl * 4 + 3], bias=self.consts[:, 0:1])
                        ins = E.activation(out=hidv(tl, w0, w1), in_=su[:, 0:w1 - w0],
                                           func=AF.Identity, bias=self.consts[:, 1:2])
                    return ins
                hq_toks.append(b.op("scalar", s_hq, waits=[vsh], fwaits=[vsh], inc="S"))
            hq_done = hq_toks[-1]
            hqT = lambda ic: self.ab(0, 13312)[:, ic * 256:(ic + 1) * 256]
            wdf = [self.ab(26, 1024), self.ab(28, 1024)]
            for pr in range(2):
                hqTt = None
                for ic in range(NIC1):
                    kw = P if ic < NIC1 - 1 else LAST1
                    for u in range(2):
                        tl = pr * 2 + u
                        fw = [b.tok("V")] if (ic == 0 and u == 0 and (pr > 0 or tb > 0)) else []
                        hqTt = self.transpose_pair(hidv(tl, ic * P, ic * P + kw),
                                                   hqT(ic)[0:kw, u * P:(u + 1) * P],
                                                   [hq_done], "vector", kr=kw, fw=fw)
                res_done = {}
                dn_prev = None
                for hf in range(2):
                    wd_toks = [None] * NIC1
                    for ic in range(NIC1):
                        kw = P if ic < NIC1 - 1 else LAST1
                        sl = ic % 2
                        w = [wd_toks[ic - 2]] if ic >= 2 else [hqTt, dn_prev]
                        b.dma("sync", wdf[sl][0:kw, :],
                              h["q_f1d"][ic * P:ic * P + kw, hf * 1024:(hf + 1) * 1024],
                              f"df1d{sl}", waits=w)
                        wl = b.tok(f"df1d{sl}")
                        for u in range(2):
                            for q in range(2):
                                def t_dn(E, ic=ic, u=u, q=q, sl=sl, kw=kw):
                                    return E.matmul(self.ps[u * 2 + q][:, :],
                                                    hqT(ic)[0:kw, u * P:(u + 1) * P],
                                                    wdf[sl][0:kw, q * 512:(q + 1) * 512],
                                                    start=(ic == 0), stop=(ic == NIC1 - 1),
                                                    skip_group_check=True)
                                w2 = []
                                if u == 0 and q == 0:
                                    w2 = [wl, hqTt]
                                    if ic == 0:
                                        pk = res_done.get(hf - 1) or (b.tok("V") if (pr > 0 or tb > 0) and hf == 0 else None)
                                        if pk is not None:
                                            w2.append(pk)
                                mt = b.op("tensor", t_dn, waits=w2,
                                          inc="T" if (u == 1 and q == 1) else None)
                        wd_toks[ic] = mt
                    dn_prev = wd_toks[NIC1 - 1]
                    dnt = b.tok("T")
                    vres = []
                    for u in range(2):
                        tl = pr * 2 + u
                        sqv = self.af(48, 1024)

                        def s_ds(E, u=u, tl=tl):
                            E.activation(out=sqv[:, 0:512], in_=self.ps[u * 2][:, :], func=AF.Identity,
                                         scale=self.fscal[:, tl * 4 + 1:tl * 4 + 2])
                            return E.activation(out=sqv[:, 512:1024], in_=self.ps[u * 2 + 1][:, :],
                                                func=AF.Identity, scale=self.fscal[:, tl * 4 + 1:tl * 4 + 2])
                        sds = b.op("scalar", s_ds, waits=[dnt] + ([b.tok("V")] if u > 0 else []), inc="S")

                        def v_rs(E, u=u, tl=tl, hf=hf, tb=tb):
                            ins = None
                            for q in range(2):
                                j = hf * 2 + q
                                x2 = self.STRM[:, (4 * j + tl) * T + tb * 512:(4 * j + tl) * T + (tb + 1) * 512]
                                ins = E.tensor_tensor(
                                    out=self.af(32 + u * 8, 2048)[:, hf * 1024 + q * 512:hf * 1024 + (q + 1) * 512],
                                    in0=sqv[:, q * 512:(q + 1) * 512], in1=x2, op=ALU.add)
                            return ins
                        vres.append(b.op("vector", v_rs, waits=[sds], inc="V"))
                    res_done[hf] = vres[-1]
                for u in range(2):
                    tl = pr * 2 + u
                    ttg = tb * 4 + tl
                    pl3 = self.af(32 + u * 8, 2048)
                    sqv2 = self.af(48, 2048)
                    outv = self.af(56, 2048)

                    def s_q2(E, pl3=pl3):
                        return E.activation(out=sqv2[:], in_=pl3[:], func=AF.Square)
                    sq2 = b.op("scalar", s_q2, waits=[res_done[1]], inc="S")

                    def v_sta(E, pl3=pl3):
                        E.tensor_reduce(out=rmx[:, 12:13], in_=pl3[:], op=ALU.add, axis=AX.X)
                        E.tensor_reduce(out=rmx[:, 13:14], in_=sqv2[:], op=ALU.add, axis=AX.X)
                        E.tensor_scalar(out=rmx[:, 12:13], in0=rmx[:, 12:13], scalar1=1.0 / T, scalar2=None, op0=ALU.mult)
                        return E.tensor_scalar(out=rmx[:, 13:14], in0=rmx[:, 13:14], scalar1=1.0 / T, scalar2=None, op0=ALU.mult)
                    vsta = b.op("vector", v_sta, waits=[sq2, res_done[1]], inc="V")

                    def v_stb(E):
                        E.tensor_mul(rmx[:, 14:15], rmx[:, 12:13], rmx[:, 12:13])
                        E.engine_nop()
                        return E.tensor_sub(rmx[:, 13:14], rmx[:, 13:14], rmx[:, 14:15])
                    vst = b.op("vector", v_stb, waits=[], fwaits=[vsta], inc="V")

                    def s_i2a(E):
                        return E.activation(out=rmx[:, 15:16], in_=rmx[:, 13:14], func=AF.Ln, bias=self.consts[:, 3:4])
                    si2a = b.op("scalar", s_i2a, waits=[vst], inc="S")

                    def s_i2b(E):
                        return E.activation(out=rmx[:, 13:14], in_=rmx[:, 15:16], func=AF.Exp, scale=-0.5)
                    si2 = b.op("scalar", s_i2b, waits=[], fwaits=[si2a], inc="S")

                    def v_out(E, pl3=pl3):
                        E.tensor_scalar(out=outv[:], in0=pl3[:], scalar1=rmx[:, 12:13], scalar2=None, op0=ALU.subtract)
                        return E.tensor_scalar(out=outv[:], in0=outv[:], scalar1=rmx[:, 13:14], scalar2=None, op0=ALU.mult)
                    vo = b.op("vector", v_out, waits=[si2], fwaits=[si2, b.tok("d_out")], inc="V")
                    out_toks.append(b.dma("gpsimd", self.out[ttg * P:(ttg + 1) * P, :], outv[:], "d_out",
                                          waits=[vo]))

    def dump_featmajor(self):
        b = self.b
        ob = self.af(0, T)
        for dc in range(NDC):
            def v_c(E, dc=dc):
                return E.tensor_copy(out=ob[:], in_=self.STRM[:, dc * T:(dc + 1) * T])
            w = [b.tok("V"), b.tok("S"), b.tok("T"), b.tok("d_out")]
            vt = b.op("vector", v_c, waits=w, fwaits=[], inc="V")
            b.dma("gpsimd", self.out[dc * P:(dc + 1) * P, :], ob[:], "d_out", waits=[vt])
        self.finish()

    def dump_packed(self):
        b = self.b
        ob = self.af(0, T)
        for ttg in range(16):
            s_, l = ttg // 4, ttg % 4

            def v_c(E, s_=s_, l=l):
                ins = None
                for j in range(4):
                    ins = E.tensor_copy(out=ob[:, j * 512:(j + 1) * 512],
                                        in_=self.STRM[:, (4 * j + l) * T + s_ * 512:(4 * j + l) * T + s_ * 512 + 512])
                return ins
            w = [b.tok("V"), b.tok("S"), b.tok("T"), b.tok("d_out")]
            vt = b.op("vector", v_c, waits=w, inc="V")
            b.dma("gpsimd", self.out[ttg * P:(ttg + 1) * P, :], ob[:], "d_out", waits=[vt])
        self.finish()

    def finish(self):
        b = self.b
        b.op("sync", lambda E: E.nop(), waits=[b.tok("d_out"), b.tok("V"), b.tok("S"), b.tok("T")])


def build_kernel(upto="f1"):
    k = Kern(upto)
    k.build()
    stack = contextlib.ExitStack()
    with stack:
        k.b.finalize(stack)
    return k.nc


def kernel(**inputs):
    shared, xT = host_prepare(inputs)
    nc = build_kernel("f1")
    in_maps = []
    for bb in range(8):
        m = dict(shared)
        m["xT"] = xT[bb]
        in_maps.append(m)
    res = run_bass_kernel_spmd(nc, in_maps, core_ids=list(range(8)))
    out = np.stack([r["out"] for r in res.results], 0)
    return np.ascontiguousarray(out.astype(np.float32))
